# version 06
# baseline (speedup 1.0000x reference)
import contextlib
import math

import numpy as np
import ml_dtypes

import concourse.bass as bass
import concourse.mybir as mybir
from concourse.bass_utils import run_bass_kernel_spmd

F32 = mybir.dt.float32
BF16 = mybir.dt.bfloat16
FP8 = mybir.dt.float8e4
AF = mybir.ActivationFunctionType
ALU = mybir.AluOpType
AX = mybir.AxisListType

D = 2048
L = 2048
B_CORE = 2
H = 16
HKV = 4
HD = 128
DFF = 8192
NDC = D // 128
NFC = DFF // 128
NTB = L // 512
NTC = L // 128
ISQ = 1.0 / math.sqrt(HD)

DEBUG = False


class Pr:
    ENGINES = ("sp", "pe", "act", "dve", "pool")

    def __init__(self):
        self.q = {e: [] for e in self.ENGINES}
        self.sems = {}
        self.cnt = {"spq": 0, "poq": 0, "pe": 0, "act": 0, "dve": 0}
        self.last_wait = {e: {} for e in self.ENGINES}

    def wait(self, eng, sem, val=None):
        v = self.cnt[sem] if val is None else val
        if v <= 0:
            return
        lw = self.last_wait[eng]
        if lw.get(sem, -1) >= v:
            return
        lw[sem] = v
        self.q[eng].append(lambda e, s=sem, v=v: e.wait_ge(self.sems[s], v))

    def emit(self, eng, fn, inc=None, by=1):
        if inc is None:
            self.q[eng].append(fn)
            return None
        self.cnt[inc] += by
        val = self.cnt[inc]

        def g(e, fn=fn, s=inc, by=by):
            fn(e).then_inc(self.sems[s], by)

        self.q[eng].append(g)
        return val

    def barrier(self):
        snap = dict(self.cnt)
        for e in self.ENGINES:
            for s, v in snap.items():
                self.wait(e, s, v)


def build_program(debug=False, dbg_b=0):
    nc = bass.Bass("TRN2", target_bir_lowering=False)

    xt = nc.dram_tensor("xt", [B_CORE, D, L], F32, kind="ExternalInput").ap()
    wq = nc.dram_tensor("wq", [D, D], F32, kind="ExternalInput").ap()
    wk = nc.dram_tensor("wk", [D, HKV * HD], F32, kind="ExternalInput").ap()
    wv = nc.dram_tensor("wv", [D, HKV * HD], F32, kind="ExternalInput").ap()
    wo = nc.dram_tensor("wo", [D, D], F32, kind="ExternalInput").ap()
    w1t = nc.dram_tensor("w1t", [D, DFF], F32, kind="ExternalInput").ap()
    w2t = nc.dram_tensor("w2t", [DFF, D], F32, kind="ExternalInput").ap()
    b1p = nc.dram_tensor("b1p", [128, NFC], F32, kind="ExternalInput").ap()
    b2p = nc.dram_tensor("b2p", [128, NDC], F32, kind="ExternalInput").ap()
    cosr = nc.dram_tensor("cosr", [128, L], BF16, kind="ExternalInput").ap()
    sinr = nc.dram_tensor("sinr", [128, L], BF16, kind="ExternalInput").ap()
    rollm = nc.dram_tensor("rollm", [128, 128], BF16, kind="ExternalInput").ap()
    eyef = nc.dram_tensor("eyef", [128, 128], F32, kind="ExternalInput").ap()
    ones128f = nc.dram_tensor("ones128f", [1, 128], F32, kind="ExternalInput").ap()
    mones128f = nc.dram_tensor("mones128f", [1, 128], F32, kind="ExternalInput").ap()
    onescb = nc.dram_tensor("onescb", [128, 1], BF16, kind="ExternalInput").ap()
    onesf32 = nc.dram_tensor("onesf32", [128, 1], F32, kind="ExternalInput").ap()
    sc24 = nc.dram_tensor("sc24", [128, 1], F32, kind="ExternalInput").ap()

    outt = nc.dram_tensor("outt", [B_CORE, D, L], F32, kind="ExternalOutput").ap()

    scratch_kind = "ExternalOutput" if debug else "Internal"
    w1q8 = nc.dram_tensor("w1q8", [D, DFF], FP8, kind=scratch_kind).ap()
    w2q8 = nc.dram_tensor("w2q8", [DFF, D], FP8, kind=scratch_kind).ap()
    xnsp = nc.dram_tensor("xnsp", [B_CORE, D, L], BF16, kind=scratch_kind).ap()
    ssp = nc.dram_tensor("ssp", [B_CORE, D, L], BF16, kind=scratch_kind).ap()
    if debug:
        d_qrot = nc.dram_tensor("d_qrot", [128, 16 * L], BF16, kind="ExternalOutput").ap()
        d_kv = nc.dram_tensor("d_kv", [128, 16512], BF16, kind="ExternalOutput").ap()
        d_misc = nc.dram_tensor("d_misc", [128, 1], F32, kind="ExternalOutput").ap()
        d_h = nc.dram_tensor("d_h", [128, 16 * L], BF16, kind="ExternalOutput").ap()
        d_g = nc.dram_tensor("d_g", [128, 8192], BF16, kind="ExternalOutput").ap()

    ctx = contextlib.ExitStack()
    sb = lambda name, shape, dt: ctx.enter_context(nc.sbuf_tensor(name, shape, dt))

    cA = sb("cA", [128, 16 * L], BF16)
    cB = sb("cB", [128, 16 * L], BF16)
    cC = sb("cC", [128, 16512], BF16)
    cD = sb("cD", [128, 7168], F32)
    c_cos = sb("c_cos", [128, L], BF16)
    c_sin = sb("c_sin", [128, L], BF16)
    c_roll = sb("c_roll", [128, 128], BF16)
    c_eyef = sb("c_eyef", [128, 128], F32)
    c_b1p = sb("c_b1p", [128, NFC], F32)
    c_b2p = sb("c_b2p", [128, NDC], F32)
    c_ones128 = sb("c_ones128", [1, 128], F32)
    c_mones128 = sb("c_mones128", [1, 128], F32)
    c_onescb = sb("c_onescb", [128, 1], BF16)
    c_onesf = sb("c_onesf", [128, 1], F32)
    c_sc24 = sb("c_sc24", [128, 1], F32)
    t_rows = sb("t_rows", [1, 2048], F32)
    t_nacc = sb("t_nacc", [1, 128], F32)
    t_acc1 = sb("t_acc1", [128, 32], F32)
    t_acc2 = sb("t_acc2", [128, 64], F32)
    t_wsum = sb("t_wsum", [128, 2], F32)
    t_th1 = sb("t_th1", [128, 1], F32)
    t_nth1 = sb("t_nth1", [128, 1], F32)
    t_th2 = sb("t_th2", [128, 1], F32)
    t_nth2 = sb("t_nth2", [128, 1], F32)
    t_thr = sb("t_thr", [1, 2], F32)
    t_cbias = sb("t_cbias", [128, 1], F32)
    t_rcp = sb("t_rcp", [128, 1], F32)
    t_eps1 = sb("t_eps1", [1, 1], F32)

    PB = ctx.enter_context(nc.psum_tensor("PB", [128, 4096], F32))
    bank = lambda i: PB[:, i * 512:(i + 1) * 512]

    sem_ctx = {}
    for s in ("spq", "poq", "pe", "act", "dve"):
        sem_ctx[s] = ctx.enter_context(nc.semaphore(s))

    P = Pr()
    P.sems = sem_ctx

    cCf = cC.ap().bitcast(F32)
    cDb = cD.ap().bitcast(BF16)
    cAa = cA.ap()
    cBa = cB.ap()
    cCa = cC.ap()
    cDa = cD.ap()

    VA0 = 8192

    def ph0():
        for dst, src in ((c_cos[:], cosr), (c_sin[:], sinr), (c_roll[:], rollm),
                         (c_eyef[:], eyef), (c_b1p[:], b1p), (c_b2p[:], b2p),
                         (c_ones128[:], ones128f), (c_mones128[:], mones128f),
                         (c_onescb[:], onescb), (c_onesf[:], onesf32),
                         (c_sc24[:], sc24)):
            P.emit("sp", lambda e, d=dst, s=src: e.dma_start(out=d, in_=s), inc="spq", by=16)
        P.emit("dve", lambda e: e.memset(t_eps1[:], 1e-5), inc="dve")

    cAf = cA.ap().bitcast(F32)
    cBf = cB.ap().bitcast(F32)

    def phA_mean_gen():
        wbuf = [cBf[:, 0:4096], cBf[:, 4096:8192]]
        loads = []
        for i in range(16):
            for half in range(2):
                loads.append((w1t[i * 128:(i + 1) * 128, half * 4096:(half + 1) * 4096],
                              t_acc1[:, i * 2 + half:i * 2 + half + 1], 4096))
        for j in range(64):
            loads.append((w2t[j * 128:(j + 1) * 128, :],
                          t_acc2[:, j:j + 1], 2048))
        red_done = []
        for idx, (src_, accdst, width) in enumerate(loads):
            bu = idx % 2
            if idx >= 2:
                P.wait("sp", "dve", red_done[idx - 2])
            P.emit("sp", lambda e, s=src_, b=bu, w=width: e.dma_start(out=wbuf[b][:, 0:w], in_=s),
                   inc="spq", by=16)
            P.wait("dve", "spq")
            v = P.emit("dve", lambda e, a=accdst, b=bu, w=width: e.tensor_reduce(
                a, wbuf[b][:, 0:w], AX.X, ALU.add, apply_absolute_value=True), inc="dve")
            red_done.append(v)
            yield

    def phA_th():
        P.emit("dve", lambda e: e.tensor_reduce(t_wsum[:, 0:1], t_acc1[:, 0:32], AX.X, ALU.add))
        P.emit("dve", lambda e: e.tensor_reduce(t_wsum[:, 1:2], t_acc2[:, 0:64], AX.X, ALU.add), inc="dve")
        P.wait("pe", "dve")
        P.wait("pe", "act")
        P.emit("pe", lambda e: e.matmul(PB[0:1, 0:1], c_sc24[:], t_wsum[:, 0:1], start=True, stop=True))
        P.emit("pe", lambda e: e.matmul(PB[0:1, 512:513], c_sc24[:], t_wsum[:, 1:2], start=True, stop=True), inc="pe")
        P.wait("dve", "pe")
        P.emit("dve", lambda e: e.tensor_scalar(t_thr[0:1, 0:1], PB[0:1, 0:1], 1e-8, 0.5, ALU.max, ALU.mult))
        P.emit("dve", lambda e: e.tensor_scalar(t_thr[0:1, 1:2], PB[0:1, 512:513], 1e-8, 0.5, ALU.max, ALU.mult), inc="dve")
        P.wait("pe", "dve")
        P.emit("pe", lambda e: e.matmul(PB[:, 0:1], c_ones128[:], t_thr[0:1, 0:1], start=True, stop=True))
        P.emit("pe", lambda e: e.matmul(PB[:, 1:2], c_mones128[:], t_thr[0:1, 0:1], start=True, stop=True))
        P.emit("pe", lambda e: e.matmul(PB[:, 512:513], c_ones128[:], t_thr[0:1, 1:2], start=True, stop=True))
        P.emit("pe", lambda e: e.matmul(PB[:, 513:514], c_mones128[:], t_thr[0:1, 1:2], start=True, stop=True), inc="pe")
        P.wait("dve", "pe")
        P.emit("dve", lambda e: e.tensor_copy(t_th1[:], PB[:, 0:1]))
        P.emit("dve", lambda e: e.tensor_copy(t_nth1[:], PB[:, 1:2]))
        P.emit("dve", lambda e: e.tensor_copy(t_th2[:], PB[:, 512:513]))
        P.emit("dve", lambda e: e.tensor_copy(t_nth2[:], PB[:, 513:514]), inc="dve")

    def phA_quant_gen():
        wbuf = [cAf[:, 0:4096], cAf[:, 4096:8192]]
        ubuf = cAa[:, 16384:20480]
        tbuf = [cAa[:, 20480:24576], cAa[:, 24576:28672]]
        qloads = []
        for i in range(16):
            for half in range(2):
                qloads.append((w1t[i * 128:(i + 1) * 128, half * 4096:(half + 1) * 4096],
                               w1q8[i * 128:(i + 1) * 128, half * 4096:(half + 1) * 4096],
                               4096, t_th1, t_nth1))
        for j in range(64):
            qloads.append((w2t[j * 128:(j + 1) * 128, :], w2q8[j * 128:(j + 1) * 128, :],
                           2048, t_th2, t_nth2))
        store_done = []
        tern_done = []
        for idx, (src_, dst, width, th, nth) in enumerate(qloads):
            bu = idx % 2
            if idx >= 2:
                P.wait("sp", "dve", tern_done[idx - 2])
            P.emit("sp", lambda e, s=src_, b=bu, w=width: e.dma_start(out=wbuf[b][:, 0:w], in_=s),
                   inc="spq", by=16)
            P.wait("dve", "spq")
            if idx >= 2:
                P.wait("dve", "poq", store_done[idx - 2])
            P.emit("dve", lambda e, b=bu, w=width, n=nth: e.tensor_scalar(
                ubuf[:, 0:w], wbuf[b][:, 0:w], n[:], None, ALU.is_lt))
            v = P.emit("dve", lambda e, b=bu, w=width, t=th: e.scalar_tensor_tensor(
                tbuf[b][:, 0:w], wbuf[b][:, 0:w], t[:], ubuf[:, 0:w], ALU.is_gt, ALU.subtract), inc="dve")
            tern_done.append(v)
            P.wait("pool", "dve")
            v = P.emit("pool", lambda e, d=dst, b=bu, w=width: e.dma_start(out=d, in_=tbuf[b][:, 0:w]),
                       inc="poq", by=16)
            store_done.append(v)
            yield

    def pump(gen, n):
        if gen is None:
            return
        for _ in range(n):
            try:
                next(gen)
            except StopIteration:
                break

    def drain(gen):
        if gen is None:
            return
        for _ in gen:
            pass

    def ph1_ln(b, pump_gen=None):
        P.barrier()
        xb = [cDa[:, 0:512], cDa[:, 512:1024]]
        xsq = cDa[:, 1024:1536]
        rstd_b = cDa[:, 1536:2048]
        mrs_b = cDa[:, 2048:2560]
        tmpf = cDa[:, 2560:3072]
        for tb in range(NTB):
            ts = slice(tb * 512, (tb + 1) * 512)
            for dc in range(NDC):
                pump(pump_gen, 1)
                bu = dc % 2
                P.wait("sp", "act")
                P.wait("sp", "pe")
                P.emit("sp", lambda e, b_=b, d=dc, t=ts, u=bu: e.dma_start(
                    out=xb[u], in_=xt[b_, d * 128:(d + 1) * 128, t]), inc="spq", by=16)
                P.wait("act", "spq")
                P.wait("act", "pe")
                va = P.emit("act", lambda e, u=bu: e.activation(xsq, xb[u], AF.Square), inc="act")
                P.wait("pe", "spq")
                if dc == 0:
                    P.wait("pe", "dve")
                P.emit("pe", lambda e, u=bu, d=dc: e.matmul(
                    PB[0:1, 0:512], c_onesf[:], xb[u], start=(d == 0), stop=(d == NDC - 1),
                    skip_group_check=True))
                P.wait("pe", "act", va)
                P.emit("pe", lambda e, d=dc: e.matmul(
                    PB[0:1, 512:1024], c_onesf[:], xsq, start=(d == 0), stop=(d == NDC - 1),
                    skip_group_check=True), inc="pe")
            P.wait("dve", "pe")
            P.emit("dve", lambda e: e.tensor_scalar(t_rows[0:1, 0:512], PB[0:1, 0:512], 1.0 / L, None, ALU.mult), inc="dve")
            P.wait("dve", "dve")
            P.emit("dve", lambda e: e.tensor_tensor(t_rows[0:1, 512:1024], t_rows[0:1, 0:512], t_rows[0:1, 0:512], ALU.mult))
            P.emit("dve", lambda e: e.tensor_scalar(t_rows[0:1, 1024:1536], PB[0:1, 512:1024], 1.0 / L, None, ALU.mult))
            P.emit("dve", lambda e: e.tensor_tensor(t_rows[0:1, 1024:1536], t_rows[0:1, 1024:1536], t_rows[0:1, 512:1024], ALU.subtract), inc="dve")
            P.wait("act", "dve")
            P.emit("act", lambda e: e.activation(t_rows[0:1, 1536:2048], t_rows[0:1, 1024:1536], AF.Ln, bias=t_eps1[:]))
            P.emit("act", lambda e: e.activation(t_rows[0:1, 1536:2048], t_rows[0:1, 1536:2048], AF.Exp, scale=-0.5), inc="act")
            P.wait("dve", "act")
            P.emit("dve", lambda e: e.tensor_tensor(t_rows[0:1, 512:1024], t_rows[0:1, 0:512], t_rows[0:1, 1536:2048], ALU.mult), inc="dve")
            P.wait("pe", "dve")
            P.emit("pe", lambda e: e.matmul(PB[:, 1024:1536], c_ones128[:], t_rows[0:1, 1536:2048], start=True, stop=True))
            P.emit("pe", lambda e: e.matmul(PB[:, 1536:2048], c_ones128[:], t_rows[0:1, 512:1024], start=True, stop=True), inc="pe")
            P.wait("dve", "pe")
            P.emit("dve", lambda e: e.tensor_copy(rstd_b, PB[:, 1024:1536]))
            P.emit("dve", lambda e: e.tensor_copy(mrs_b, PB[:, 1536:2048]), inc="dve")
            for dc in range(NDC):
                pump(pump_gen, 1)
                bu = dc % 2
                P.wait("sp", "dve")
                P.emit("sp", lambda e, b_=b, d=dc, t=ts, u=bu: e.dma_start(
                    out=xb[u], in_=xt[b_, d * 128:(d + 1) * 128, t]), inc="spq", by=16)
                P.wait("dve", "spq")
                P.emit("dve", lambda e, u=bu: e.tensor_tensor(tmpf, xb[u], rstd_b, ALU.mult))
                P.emit("dve", lambda e, d=dc, t=ts: e.tensor_tensor(
                    cAa[:, d * 2048 + t.start:d * 2048 + t.stop], tmpf, mrs_b, ALU.subtract), inc="dve")
            P.wait("sp", "dve")
            xn_view = cAa.rearrange("p (dc t) -> p dc t", dc=NDC)[:, :, ts]
            sp_view = xnsp[b].rearrange("(dc p) t -> p dc t", p=128)[:, :, ts]
            for q in range(4):
                P.emit("sp", lambda e, q=q, xv=xn_view, sv=sp_view: e.dma_start(
                    out=sv[:, q * 4:(q + 1) * 4, :], in_=xv[:, q * 4:(q + 1) * 4, :]), inc="spq", by=16)

    def ph1_v(b, pump_gen=None):
        P.barrier()
        for tc in range(16):
            for kv in range(4):
                col = VA0 + tc * 520 + kv * 130 + 128
                P.emit("dve", lambda e, c=col: e.memset(cCa[:, c:c + 1], 1.0))
        P.emit("dve", lambda e: e.engine_nop(), inc="dve")
        vw = [cDb[:, 6144:6656], cDb[:, 6656:7168]]
        mm_done = []
        for half in range(2):
            for dc in range(NDC):
                pump(pump_gen, 1)
                bu = dc % 2
                gi = half * NDC + dc
                if gi >= 2:
                    P.wait("pool", "pe", mm_done[gi - 2])
                P.emit("pool", lambda e, d=dc, u=bu: e.dma_start(
                    out=vw[u], in_=wv[d * 128:(d + 1) * 128, :]), inc="poq", by=16)
                P.wait("pe", "poq")
                if dc == 0:
                    P.wait("pe", "dve")
                for tcl in range(8):
                    tc = half * 8 + tcl
                    v = P.emit("pe", lambda e, d=dc, t=tc, tl=tcl, u=bu: e.matmul(
                        bank(tl), cAa[:, d * 2048 + t * 128:d * 2048 + (t + 1) * 128],
                        vw[u], start=(d == 0), stop=(d == NDC - 1),
                        skip_group_check=True),
                        inc="pe" if tcl == 7 else None)
                    if tcl == 7:
                        mm_done.append(v)
            P.wait("dve", "pe")
            for tcl in range(8):
                tc = half * 8 + tcl
                dst = cCa[:, VA0 + tc * 520:VA0 + (tc + 1) * 520].rearrange(
                    "p (kv c) -> p kv c", kv=4)[:, :, 0:128]
                src = bank(tcl).rearrange("p (kv c) -> p kv c", kv=4)
                P.emit("dve", lambda e, d=dst, s=src: e.tensor_copy(d, s), inc="dve")

    def ph1_qk(b):
        P.barrier()
        wqb = [cDb[:, 0:2048], cDb[:, 2048:4096]]
        qcos = cDb[:, 4096:4608]
        qraw = cDb[:, 4608:5120]
        tmp2 = cDb[:, 5120:5632]
        sqt = cDb[:, 5632:6144]
        for ec in range(H + HKV):
            isq = ec < H
            w_src = wq if isq else wk
            ecs = ec if isq else ec - H
            dstbase = (cBa, ec * 2048) if isq else (cCa, ecs * 2048)
            bu = ec % 2
            wr = w_src.rearrange("(dc p) e -> p dc e", p=128)[:, :, ecs * 128:(ecs + 1) * 128]
            wv_ = wqb[bu].rearrange("p (dc e) -> p dc e", dc=NDC)
            if ec >= 2:
                P.wait("pool", "pe")
            for q in range(4):
                P.emit("pool", lambda e, q=q, w=wr, o=wv_: e.dma_start(
                    out=o[:, q * 4:(q + 1) * 4, :], in_=w[:, q * 4:(q + 1) * 4, :]), inc="poq", by=16)
            P.wait("pe", "poq")
            P.wait("pe", "dve")
            for dc in range(NDC):
                for tb in range(NTB):
                    P.emit("pe", lambda e, d=dc, t=tb, u=bu: e.matmul(
                        bank(t), wqb[u][:, d * 128:(d + 1) * 128],
                        cAa[:, d * 2048 + t * 512:d * 2048 + (t + 1) * 512],
                        start=(d == 0), stop=(d == NDC - 1), skip_group_check=True),
                        inc="pe" if (dc == NDC - 1 and tb == NTB - 1) else None)
            for tb in range(NTB):
                ts0, ts1 = tb * 512, (tb + 1) * 512
                P.wait("dve", "pe")
                P.emit("dve", lambda e, t0=ts0, t1=ts1, t=tb: e.tensor_tensor(
                    qcos, bank(t), c_cos[:, t0:t1], ALU.mult))
                P.emit("dve", lambda e, t=tb: e.tensor_copy(qraw, bank(t)), inc="dve")
                P.wait("pe", "dve")
                rb = 4 + tb % 2
                P.emit("pe", lambda e, r=rb: e.matmul(
                    bank(r), c_roll[:], qraw, start=True, stop=True), inc="pe")
                P.wait("dve", "pe")
                P.emit("dve", lambda e, r=rb, t0=ts0, t1=ts1: e.tensor_tensor(
                    tmp2, bank(r), c_sin[:, t0:t1], ALU.mult))
                arr, base = dstbase
                P.emit("dve", lambda e, a=arr, o=base + ts0: e.tensor_tensor(
                    a[:, o:o + 512], tmp2, qcos, ALU.add), inc="dve")
                P.wait("act", "dve")
                P.wait("act", "pe")
                P.emit("act", lambda e, a=arr, o=base + ts0: e.activation(
                    sqt, a[:, o:o + 512], AF.Square), inc="act")
                P.wait("pe", "act")
                P.emit("pe", lambda e: e.matmul(PB[0:1, 3072:3584], c_onescb[:], sqt,
                                                start=True, stop=True), inc="pe")
                P.wait("dve", "pe")
                nidx = (ec * 4 + tb) if isq else (64 + ecs * 4 + tb)
                P.emit("dve", lambda e, n=nidx: e.tensor_reduce(
                    t_nacc[0:1, n:n + 1], PB[0:1, 3072:3584], AX.X, ALU.max), inc="dve")
        P.emit("dve", lambda e: e.tensor_reduce(t_nacc[0:1, 96:97], t_nacc[0:1, 0:64], AX.X, ALU.max))
        P.emit("dve", lambda e: e.tensor_reduce(t_nacc[0:1, 97:98], t_nacc[0:1, 64:80], AX.X, ALU.max), inc="dve")
        P.wait("dve", "dve")
        P.emit("dve", lambda e: e.tensor_tensor(t_nacc[0:1, 98:99], t_nacc[0:1, 96:97], t_nacc[0:1, 97:98], ALU.mult), inc="dve")
        P.wait("act", "dve")
        P.emit("act", lambda e: e.activation(t_nacc[0:1, 99:100], t_nacc[0:1, 98:99], AF.Sqrt, scale=1.0 / HD), inc="act")
        P.wait("pe", "act")
        P.emit("pe", lambda e: e.matmul(PB[:, 3072:3073], c_mones128[:], t_nacc[0:1, 99:100], start=True, stop=True), inc="pe")
        P.wait("dve", "pe")
        P.emit("dve", lambda e: e.tensor_copy(t_cbias[:], PB[:, 3072:3073]), inc="dve")

    def ph2(b, pump_gen=None):
        P.barrier()
        PT = cDb
        rb_sbuf = cDa[:, 4096:4608]
        t_prow = t_rows[0:1, 0:512]
        exp_after = {}
        av_evac = {0: 0, 1: 0}
        denom_free = 0
        bcast_free = 0
        for h in range(H):
            kv = h // 4
            for qb in range(NTB):
                q0 = qb * 512
                ab = 4 + (qb % 2)
                av_done = den_done = None
                exp_val = {}
                for g in range(9):
                    if g < 8:
                        bs = g % 2
                        b0 = bs * 2
                        P.wait("pe", "act", exp_after.get(bs, 0))
                        for ki in range(2):
                            kc = g * 2 + ki
                            P.emit("pe", lambda e, k=kc, v=kv, bk=b0 + ki, q=q0, h=h: e.matmul(
                                bank(bk), cCa[:, v * 2048 + k * 128:v * 2048 + (k + 1) * 128],
                                cBa[:, h * 2048 + q:h * 2048 + q + 512],
                                start=True, stop=True, skip_group_check=True), inc="pe")
                        P.wait("act", "pe")
                        v = P.emit("act", lambda e, g=g, b0=b0: e.activation(
                            PT[:, g * 1024:(g + 1) * 1024], PB[:, b0 * 512:(b0 + 2) * 512],
                            AF.Exp, scale=ISQ, bias=t_cbias[:]), inc="act")
                        exp_after[bs] = v
                        exp_val[g] = v
                    if g >= 1:
                        ga = g - 1
                        P.wait("pe", "act", exp_val[ga])
                        if ga == 0:
                            P.wait("pe", "dve", max(av_evac[qb % 2], denom_free))
                        for ki in range(2):
                            kc = ga * 2 + ki
                            av_done = P.emit("pe", lambda e, k=kc, v=kv, a=ab: e.matmul(
                                PB[:, a * 512:(a + 1) * 512],
                                cCa[:, VA0 + k * 520 + v * 130:VA0 + k * 520 + v * 130 + 128],
                                PT[:, k * 512:(k + 1) * 512],
                                start=(k == 0), stop=(k == 15), skip_group_check=True),
                                inc="pe" if kc == 15 else None)
                            den_done = P.emit("pe", lambda e, k=kc, v=kv: e.matmul(
                                PB[0:1, 3072:3584],
                                cCa[:, VA0 + k * 520 + v * 130 + 128:VA0 + k * 520 + v * 130 + 129],
                                PT[:, k * 512:(k + 1) * 512],
                                start=(k == 0), stop=(k == 15), skip_group_check=True),
                                inc="pe" if kc == 15 else None)
                pump(pump_gen, 2)
                P.wait("dve", "pe", den_done)
                denom_free = P.emit("dve", lambda e: e.reciprocal(t_prow, PB[0:1, 3072:3584]), inc="dve")
                P.wait("pe", "dve", max(denom_free, bcast_free))
                bc_done = P.emit("pe", lambda e: e.matmul(
                    PB[:, 3584:4096], c_ones128[:], t_prow, start=True, stop=True), inc="pe")
                P.wait("dve", "pe", bc_done)
                P.emit("dve", lambda e: e.tensor_copy(rb_sbuf, PB[:, 3584:4096]), inc="dve")
                bcast_free = P.cnt["dve"]
                P.wait("dve", "pe", av_done)
                P.wait("dve", "dve")
                av_evac[qb % 2] = P.emit("dve", lambda e, a=ab, h=h, q=q0: e.tensor_tensor(
                    cBa[:, h * 2048 + q:h * 2048 + q + 512], PB[:, a * 512:(a + 1) * 512],
                    rb_sbuf, ALU.mult), inc="dve")
        if debug and b == dbg_b:
            P.barrier()
            P.emit("sp", lambda e: e.dma_start(out=d_qrot, in_=cBa[:, :]), inc="spq", by=16)
            P.emit("sp", lambda e: e.dma_start(out=d_kv, in_=cCa[:, :]), inc="spq", by=16)
            P.emit("sp", lambda e: e.dma_start(out=d_misc, in_=t_cbias[:]), inc="spq", by=16)

    def ph25(b, pump_gen=None):
        P.barrier()
        wob = [cDb[:, 0:2048], cDb[:, 2048:4096]]
        stile = [cDb[:, 4096:4608], cDb[:, 4608:5120]]
        mm_last = {}
        ld_last = {}
        evac_last = {}
        store_last = {}
        for dc in range(NDC):
            bu = dc % 2
            wr = wo.rearrange("(ec p) d -> p ec d", p=128)[:, :, dc * 128:(dc + 1) * 128]
            wv_ = wob[bu].rearrange("p (ec d) -> p ec d", ec=16)
            if dc >= 2:
                P.wait("pool", "pe", mm_last[dc - 2])
            for q in range(4):
                ld_last[dc] = P.emit("pool", lambda e, q=q, w=wr, o=wv_: e.dma_start(
                    out=o[:, q * 4:(q + 1) * 4, :], in_=w[:, q * 4:(q + 1) * 4, :]), inc="poq", by=16)
            P.wait("pe", "poq", ld_last[dc])
            if dc >= 1:
                for t in range(NTB):
                    P.wait("pe", "dve", evac_last[(dc - 1, t)])
            mm_vals = {}
            for tb in range(NTB):
                for ec in range(16):
                    v = P.emit("pe", lambda e, t=tb, c=ec, u=bu: e.matmul(
                        bank(t), wob[u][:, c * 128:(c + 1) * 128],
                        cBa[:, c * 2048 + t * 512:(c) * 2048 + (t + 1) * 512],
                        start=(c == 0), stop=(c == 15), skip_group_check=True),
                        inc="pe" if ec == 15 else None)
                    if v is not None:
                        mm_vals[tb] = v
            mm_last[dc] = mm_vals[NTB - 1]
            pump(pump_gen, 6)
            for tb in range(NTB):
                su = tb % 2
                P.wait("dve", "pe", mm_vals[tb])
                if tb >= 2:
                    P.wait("dve", "spq", store_last[(dc, tb - 2)])
                evac_last[(dc, tb)] = P.emit("dve", lambda e, t=tb, u=su: e.tensor_copy(stile[u], bank(t)), inc="dve")
                P.wait("sp", "dve", evac_last[(dc, tb)])
                store_last[(dc, tb)] = P.emit("sp", lambda e, b_=b, d=dc, t=tb, u=su: e.dma_start(
                    out=ssp[b_, d * 128:(d + 1) * 128, t * 512:(t + 1) * 512], in_=stile[u]),
                    inc="spq", by=16)

    def ph3(b):
        P.barrier()
        xn_tb = cDb[:, 0:8192]
        gtile = [cDb[:, 8192:8704], cDb[:, 8704:9216]]
        sti = [cDb[:, 9216:9728], cDb[:, 9728:10240]]
        xg = [cDa[:, 5120:5632], cDa[:, 5632:6144]]
        sgt = cDa[:, 6144:6656]
        otile = cDa[:, 6656:7168]
        chunk = [cCa[:, 0:8192], cCa[:, 8192:16384]]
        for tb in range(NTB):
            ts0 = tb * 512
            P.wait("sp", "pe")
            xv = xnsp[b].rearrange("(dc p) t -> p dc t", p=128)[:, :, ts0:ts0 + 512]
            ov = xn_tb.rearrange("p (dc t) -> p dc t", dc=NDC)
            for q in range(4):
                P.emit("sp", lambda e, q=q, x=xv, o=ov: e.dma_start(
                    out=o[:, q * 4:(q + 1) * 4, :], in_=x[:, q * 4:(q + 1) * 4, :]), inc="spq", by=16)
            w1r = w1q8.rearrange("(dc p) f -> p dc f", p=128)
            gelu_done = {}
            ld_done = {}
            for fch in range(16):
                bu = fch % 2
                if fch >= 2:
                    P.wait("pool", "pe", ld_done.get(fch - 2, 0))
                else:
                    P.wait("pool", "pe")
                cv = chunk[bu].rearrange("p (dc f) -> p dc f", dc=NDC)
                for q in range(4):
                    P.emit("pool", lambda e, q=q, f=fch, c=cv: e.dma_start(
                        out=c[:, q * 4:(q + 1) * 4, :],
                        in_=w1r[:, q * 4:(q + 1) * 4, f * 512:(f + 1) * 512]), inc="poq", by=16)
                P.wait("pe", "poq")
                P.wait("pe", "spq")
                for fcs in range(4):
                    fc = fch * 4 + fcs
                    bk = fc % 3
                    P.wait("pe", "act", gelu_done.get(bk, 0))
                    for dc in range(NDC):
                        v = P.emit("pe", lambda e, d=dc, s=fcs, k=bk, u=bu: e.matmul(
                            bank(k), chunk[u].rearrange("p (dc f) -> p dc f", dc=NDC)[:, d, s * 128:(s + 1) * 128],
                            xn_tb[:, d * 512:(d + 1) * 512],
                            start=(d == 0), stop=(d == NDC - 1), skip_group_check=True),
                            inc="pe" if dc == NDC - 1 else None)
                        if v is not None:
                            ld_done[fch] = v
                    P.wait("act", "pe")
                    gelu_done[bk] = P.emit("act", lambda e, f=fc, k=bk: e.activation(
                        cAa[:, f * 512:(f + 1) * 512], bank(k), AF.Gelu,
                        bias=c_b1p[:, f:f + 1]), inc="act")
            if debug and b == dbg_b and tb == 0:
                P.wait("sp", "act")
                P.emit("sp", lambda e: e.dma_start(out=d_h, in_=cAa[:, :]), inc="spq", by=16)
            w2r = w2q8.rearrange("(fc p) d -> p fc d", p=128)
            ld2_done = {}
            for dg in range(4):
                for f2c in range(4):
                    bu = f2c % 2
                    gi = dg * 4 + f2c
                    if gi >= 2:
                        P.wait("pool", "pe", ld2_done[gi - 2])
                    else:
                        P.wait("pool", "pe")
                    cv = chunk[bu].rearrange("p (fcs d) -> p fcs d", fcs=16)
                    for q in range(4):
                        P.emit("pool", lambda e, q=q, f=f2c, g=dg, c=cv: e.dma_start(
                            out=c[:, q * 4:(q + 1) * 4, :],
                            in_=w2r[:, f * 16 + q * 4:f * 16 + (q + 1) * 4, g * 512:(g + 1) * 512]),
                            inc="poq", by=16)
                    P.wait("pe", "poq")
                    if f2c == 0:
                        P.wait("pe", "act")
                    for fcs in range(16):
                        fcg = f2c * 16 + fcs
                        for di in range(4):
                            v = P.emit("pe", lambda e, s=fcs, i=di, g=fcg, u=bu: e.matmul(
                                bank(4 + i), chunk[u].rearrange("p (fcs d) -> p fcs d", fcs=16)[:, s, i * 128:(i + 1) * 128],
                                cAa[:, g * 512:(g + 1) * 512],
                                start=(g == 0), stop=(g == 63), skip_group_check=True),
                                inc="pe" if (fcs == 15 and di == 3) else None)
                            if v is not None:
                                ld2_done[gi] = v
                for di in range(4):
                    dc = dg * 4 + di
                    gu = di % 2
                    P.wait("sp", "dve")
                    P.emit("sp", lambda e, b_=b, d=dc, t=ts0, u=gu: e.dma_start(
                        out=xg[u], in_=xt[b_, d * 128:(d + 1) * 128, t:t + 512]), inc="spq", by=16)
                    P.emit("sp", lambda e, b_=b, d=dc, t=ts0, u=gu: e.dma_start(
                        out=sti[u], in_=ssp[b_, d * 128:(d + 1) * 128, t:t + 512]), inc="spq", by=16)
                    P.wait("act", "pe")
                    if di >= 2:
                        P.wait("act", "dve")
                    P.emit("act", lambda e, d=dc, i=di, u=gu: e.activation(
                        gtile[u], bank(4 + i), AF.Silu, bias=c_b2p[:, d:d + 1]), inc="act")
                    if debug and b == dbg_b and tb == 0:
                        P.wait("sp", "act")
                        P.emit("sp", lambda e, d=dc, u=gu: e.dma_start(
                            out=d_g[:, d * 512:(d + 1) * 512], in_=gtile[u]), inc="spq", by=16)
                        P.wait("dve", "spq")
                    P.wait("dve", "act")
                    P.wait("dve", "spq")
                    P.emit("dve", lambda e, u=gu: e.tensor_tensor(sgt, gtile[u], sti[u], ALU.mult))
                    P.emit("dve", lambda e, u=gu: e.tensor_tensor(otile, sgt, xg[u], ALU.add), inc="dve")
                    P.wait("sp", "dve")
                    P.emit("sp", lambda e, b_=b, d=dc, t=ts0: e.dma_start(
                        out=outt[b_, d * 128:(d + 1) * 128, t:t + 512], in_=otile), inc="spq", by=16)

    ph0()
    mg = phA_mean_gen()
    ph1_ln(0, pump_gen=mg)
    ph1_v(0, pump_gen=mg)
    drain(mg)
    ph1_qk(0)
    phA_th()
    qg = phA_quant_gen()
    ph2(0, pump_gen=qg)
    ph25(0, pump_gen=qg)
    drain(qg)
    ph3(0)
    ph1_ln(1)
    ph1_v(1)
    ph1_qk(1)
    ph2(1)
    ph25(1)
    ph3(1)
    P.barrier()

    with nc.Block() as block:
        @block.sync
        def _(e):
            for f in P.q["sp"]:
                f(e)

        @block.tensor
        def _(e):
            for f in P.q["pe"]:
                f(e)

        @block.scalar
        def _(e):
            for f in P.q["act"]:
                f(e)

        @block.vector
        def _(e):
            for f in P.q["dve"]:
                f(e)

        @block.gpsimd
        def _(e):
            for f in P.q["pool"]:
                f(e)

    ctx.close()
    return nc


def _host_constants():
    pos = np.arange(L, dtype=np.float64)
    inv_freq = 1.0 / (10000.0 ** (np.arange(0, HD, 2, dtype=np.float64) / HD))
    freqs = pos[:, None] * inv_freq[None, :]
    emb = np.concatenate([freqs, freqs], axis=-1)
    cos_fm = np.cos(emb).T.astype(ml_dtypes.bfloat16)
    sin_fm = np.sin(emb).T.astype(ml_dtypes.bfloat16)
    roll = np.zeros((128, 128), dtype=ml_dtypes.bfloat16)
    for j in range(128):
        roll[j, (j + 1) % 128] = 1.0
    return {
        "cosr": np.ascontiguousarray(cos_fm),
        "sinr": np.ascontiguousarray(sin_fm),
        "rollm": roll,
        "eyef": np.eye(128, dtype=np.float32),
        "ones128f": np.ones((1, 128), dtype=np.float32),
        "mones128f": np.full((1, 128), -1.0, dtype=np.float32),
        "onescb": np.ones((128, 1), dtype=ml_dtypes.bfloat16),
        "onesf32": np.ones((128, 1), dtype=np.float32),
        "sc24": np.full((128, 1), 2.0 ** -24, dtype=np.float32),
    }


def kernel(L0_attn_x, L0_attn_Wq, L0_attn_Wk, L0_attn_Wv, L0_attn_Wo,
           L0_ffn_w1, L0_ffn_b1, L0_ffn_w2, L0_ffn_b2):
    n_cores = 8
    x = np.asarray(L0_attn_x)
    consts = _host_constants()
    common = {
        "wq": np.ascontiguousarray(L0_attn_Wq, dtype=np.float32),
        "wk": np.ascontiguousarray(L0_attn_Wk, dtype=np.float32),
        "wv": np.ascontiguousarray(L0_attn_Wv, dtype=np.float32),
        "wo": np.ascontiguousarray(L0_attn_Wo, dtype=np.float32),
        "w1t": np.ascontiguousarray(np.asarray(L0_ffn_w1).T),
        "w2t": np.ascontiguousarray(np.asarray(L0_ffn_w2).T),
        "b1p": np.ascontiguousarray(np.asarray(L0_ffn_b1).reshape(NFC, 128).T),
        "b2p": np.ascontiguousarray(np.asarray(L0_ffn_b2).reshape(NDC, 128).T),
        **consts,
    }
    in_maps = []
    for c in range(n_cores):
        xs = x[c * B_CORE:(c + 1) * B_CORE]
        xtc = np.ascontiguousarray(np.transpose(xs, (0, 2, 1)))
        in_maps.append({"xt": xtc, **common})

    nc = build_program(debug=DEBUG)
    res = run_bass_kernel_spmd(nc, in_maps, core_ids=list(range(n_cores)))

    out = np.empty((16, L, D), dtype=np.float32)
    for c in range(n_cores):
        ot = res.results[c]["outt"]
        out[c * B_CORE:(c + 1) * B_CORE] = np.transpose(ot, (0, 2, 1))
    return out


# version 10
# speedup vs baseline: 1.2359x; 1.2359x over previous
import contextlib
import math

import numpy as np
import ml_dtypes

import concourse.bass as bass
import concourse.mybir as mybir
from concourse.bass_utils import run_bass_kernel_spmd

F32 = mybir.dt.float32
BF16 = mybir.dt.bfloat16
FP8 = mybir.dt.float8e4
AF = mybir.ActivationFunctionType
ALU = mybir.AluOpType
AX = mybir.AxisListType

D = 2048
L = 2048
B_CORE = 2
H = 16
HKV = 4
HD = 128
DFF = 8192
NDC = D // 128
NFC = DFF // 128
NTB = L // 512
NTC = L // 128
ISQ = 1.0 / math.sqrt(HD)

DEBUG = False


class Pr:
    ENGINES = ("sp", "pe", "act", "dve", "pool")

    def __init__(self):
        self.q = {e: [] for e in self.ENGINES}
        self.sems = {}
        self.cnt = {"spq": 0, "poq": 0, "pe": 0, "act": 0, "dve": 0}
        self.last_wait = {e: {} for e in self.ENGINES}

    def wait(self, eng, sem, val=None):
        v = self.cnt[sem] if val is None else val
        if v <= 0:
            return
        lw = self.last_wait[eng]
        if lw.get(sem, -1) >= v:
            return
        lw[sem] = v
        self.q[eng].append(lambda e, s=sem, v=v: e.wait_ge(self.sems[s], v))

    def emit(self, eng, fn, inc=None, by=1):
        if inc is None:
            self.q[eng].append(fn)
            return None
        self.cnt[inc] += by
        val = self.cnt[inc]

        def g(e, fn=fn, s=inc, by=by):
            fn(e).then_inc(self.sems[s], by)

        self.q[eng].append(g)
        return val

    def barrier(self):
        snap = dict(self.cnt)
        for e in self.ENGINES:
            for s, v in snap.items():
                self.wait(e, s, v)


def build_program(debug=False, dbg_b=0):
    nc = bass.Bass("TRN2", target_bir_lowering=False)

    xt = nc.dram_tensor("xt", [B_CORE, D, L], F32, kind="ExternalInput").ap()
    wq = nc.dram_tensor("wq", [D, D], F32, kind="ExternalInput").ap()
    wk = nc.dram_tensor("wk", [D, HKV * HD], F32, kind="ExternalInput").ap()
    wv = nc.dram_tensor("wv", [D, HKV * HD], F32, kind="ExternalInput").ap()
    wo = nc.dram_tensor("wo", [D, D], F32, kind="ExternalInput").ap()
    w1t = nc.dram_tensor("w1t", [D, DFF], F32, kind="ExternalInput").ap()
    w2t = nc.dram_tensor("w2t", [DFF, D], F32, kind="ExternalInput").ap()
    b1p = nc.dram_tensor("b1p", [128, NFC], F32, kind="ExternalInput").ap()
    b2p = nc.dram_tensor("b2p", [128, NDC], F32, kind="ExternalInput").ap()
    cosr = nc.dram_tensor("cosr", [128, L], BF16, kind="ExternalInput").ap()
    sinr = nc.dram_tensor("sinr", [128, L], BF16, kind="ExternalInput").ap()
    rollm = nc.dram_tensor("rollm", [128, 128], BF16, kind="ExternalInput").ap()
    eyef = nc.dram_tensor("eyef", [128, 128], F32, kind="ExternalInput").ap()
    ones128f = nc.dram_tensor("ones128f", [1, 128], F32, kind="ExternalInput").ap()
    mones128f = nc.dram_tensor("mones128f", [1, 128], F32, kind="ExternalInput").ap()
    onescb = nc.dram_tensor("onescb", [128, 1], BF16, kind="ExternalInput").ap()
    onesf32 = nc.dram_tensor("onesf32", [128, 1], F32, kind="ExternalInput").ap()
    sc24 = nc.dram_tensor("sc24", [128, 1], F32, kind="ExternalInput").ap()

    outt = nc.dram_tensor("outt", [B_CORE, D, L], F32, kind="ExternalOutput").ap()

    scratch_kind = "ExternalOutput" if debug else "Internal"
    w1q8 = nc.dram_tensor("w1q8", [D, DFF], FP8, kind=scratch_kind).ap()
    w2q8 = nc.dram_tensor("w2q8", [DFF, D], FP8, kind=scratch_kind).ap()
    xnsp = nc.dram_tensor("xnsp", [B_CORE, D, L], BF16, kind=scratch_kind).ap()
    ssp = nc.dram_tensor("ssp", [B_CORE, D, L], BF16, kind=scratch_kind).ap()
    if debug:
        d_qrot = nc.dram_tensor("d_qrot", [128, 16 * L], BF16, kind="ExternalOutput").ap()
        d_kv = nc.dram_tensor("d_kv", [128, 16512], BF16, kind="ExternalOutput").ap()
        d_misc = nc.dram_tensor("d_misc", [128, 1], F32, kind="ExternalOutput").ap()
        d_h = nc.dram_tensor("d_h", [128, 16 * L], BF16, kind="ExternalOutput").ap()
        d_g = nc.dram_tensor("d_g", [128, 8192], BF16, kind="ExternalOutput").ap()

    ctx = contextlib.ExitStack()
    sb = lambda name, shape, dt: ctx.enter_context(nc.sbuf_tensor(name, shape, dt))

    cA = sb("cA", [128, 16 * L], BF16)
    cB = sb("cB", [128, 16 * L], BF16)
    cC = sb("cC", [128, 16512], BF16)
    cD = sb("cD", [128, 7168], F32)
    c_cos = sb("c_cos", [128, L], BF16)
    c_sin = sb("c_sin", [128, L], BF16)
    c_roll = sb("c_roll", [128, 128], BF16)
    c_eyef = sb("c_eyef", [128, 128], F32)
    c_b1p = sb("c_b1p", [128, NFC], F32)
    c_b2p = sb("c_b2p", [128, NDC], F32)
    c_ones128 = sb("c_ones128", [1, 128], F32)
    c_mones128 = sb("c_mones128", [1, 128], F32)
    c_onescb = sb("c_onescb", [128, 1], BF16)
    c_onesf = sb("c_onesf", [128, 1], F32)
    c_sc24 = sb("c_sc24", [128, 1], F32)
    t_rows = sb("t_rows", [1, 2048], F32)
    t_nacc = sb("t_nacc", [1, 128], F32)
    t_acc1 = sb("t_acc1", [128, 32], F32)
    t_acc2 = sb("t_acc2", [128, 64], F32)
    t_wsum = sb("t_wsum", [128, 2], F32)
    t_th1 = sb("t_th1", [128, 1], F32)
    t_nth1 = sb("t_nth1", [128, 1], F32)
    t_th2 = sb("t_th2", [128, 1], F32)
    t_nth2 = sb("t_nth2", [128, 1], F32)
    t_thr = sb("t_thr", [1, 2], F32)
    t_cbias = sb("t_cbias", [128, 1], F32)
    t_rcp = sb("t_rcp", [128, 1], F32)
    t_eps1 = sb("t_eps1", [1, 1], F32)

    PB = ctx.enter_context(nc.psum_tensor("PB", [128, 4096], F32))
    bank = lambda i: PB[:, i * 512:(i + 1) * 512]

    sem_ctx = {}
    for s in ("spq", "poq", "pe", "act", "dve"):
        sem_ctx[s] = ctx.enter_context(nc.semaphore(s))

    P = Pr()
    P.sems = sem_ctx

    cCf = cC.ap().bitcast(F32)
    cDb = cD.ap().bitcast(BF16)
    cAa = cA.ap()
    cBa = cB.ap()
    cCa = cC.ap()
    cDa = cD.ap()

    VA0 = 8192

    def ph0():
        for dst, src in ((c_cos[:], cosr), (c_sin[:], sinr), (c_roll[:], rollm),
                         (c_eyef[:], eyef), (c_b1p[:], b1p), (c_b2p[:], b2p),
                         (c_ones128[:], ones128f), (c_mones128[:], mones128f),
                         (c_onescb[:], onescb), (c_onesf[:], onesf32),
                         (c_sc24[:], sc24)):
            P.emit("sp", lambda e, d=dst, s=src: e.dma_start(out=d, in_=s), inc="spq", by=16)
        P.emit("dve", lambda e: e.memset(t_eps1[:], 1e-5), inc="dve")

    cAf = cA.ap().bitcast(F32)
    cBf = cB.ap().bitcast(F32)

    def phA_mean_gen():
        wbuf = [cBf[:, 0:4096], cBf[:, 4096:8192]]
        loads = []
        for i in range(16):
            for half in range(2):
                loads.append((w1t[i * 128:(i + 1) * 128, half * 4096:(half + 1) * 4096],
                              t_acc1[:, i * 2 + half:i * 2 + half + 1], 4096))
        for j in range(64):
            loads.append((w2t[j * 128:(j + 1) * 128, :],
                          t_acc2[:, j:j + 1], 2048))
        red_done = []
        for idx, (src_, accdst, width) in enumerate(loads):
            bu = idx % 2
            if idx >= 2:
                P.wait("sp", "dve", red_done[idx - 2])
            P.emit("sp", lambda e, s=src_, b=bu, w=width: e.dma_start(out=wbuf[b][:, 0:w], in_=s),
                   inc="spq", by=16)
            P.wait("dve", "spq")
            v = P.emit("dve", lambda e, a=accdst, b=bu, w=width: e.tensor_reduce(
                a, wbuf[b][:, 0:w], AX.X, ALU.add, apply_absolute_value=True), inc="dve")
            red_done.append(v)
            yield

    def phA_th():
        P.emit("dve", lambda e: e.tensor_reduce(t_wsum[:, 0:1], t_acc1[:, 0:32], AX.X, ALU.add))
        P.emit("dve", lambda e: e.tensor_reduce(t_wsum[:, 1:2], t_acc2[:, 0:64], AX.X, ALU.add), inc="dve")
        P.wait("pe", "dve")
        P.wait("pe", "act")
        P.emit("pe", lambda e: e.matmul(PB[0:1, 0:1], c_sc24[:], t_wsum[:, 0:1], start=True, stop=True))
        P.emit("pe", lambda e: e.matmul(PB[0:1, 512:513], c_sc24[:], t_wsum[:, 1:2], start=True, stop=True), inc="pe")
        P.wait("dve", "pe")
        P.emit("dve", lambda e: e.tensor_scalar(t_thr[0:1, 0:1], PB[0:1, 0:1], 1e-8, 0.5, ALU.max, ALU.mult))
        P.emit("dve", lambda e: e.tensor_scalar(t_thr[0:1, 1:2], PB[0:1, 512:513], 1e-8, 0.5, ALU.max, ALU.mult), inc="dve")
        P.wait("pe", "dve")
        P.emit("pe", lambda e: e.matmul(PB[:, 0:1], c_ones128[:], t_thr[0:1, 0:1], start=True, stop=True))
        P.emit("pe", lambda e: e.matmul(PB[:, 1:2], c_mones128[:], t_thr[0:1, 0:1], start=True, stop=True))
        P.emit("pe", lambda e: e.matmul(PB[:, 512:513], c_ones128[:], t_thr[0:1, 1:2], start=True, stop=True))
        P.emit("pe", lambda e: e.matmul(PB[:, 513:514], c_mones128[:], t_thr[0:1, 1:2], start=True, stop=True), inc="pe")
        P.wait("dve", "pe")
        P.emit("dve", lambda e: e.tensor_copy(t_th1[:], PB[:, 0:1]))
        P.emit("dve", lambda e: e.tensor_copy(t_nth1[:], PB[:, 1:2]))
        P.emit("dve", lambda e: e.tensor_copy(t_th2[:], PB[:, 512:513]))
        P.emit("dve", lambda e: e.tensor_copy(t_nth2[:], PB[:, 513:514]), inc="dve")

    def phA_quant_gen():
        wbuf = [cAf[:, 0:4096], cAf[:, 4096:8192]]
        ubuf = cAa[:, 16384:20480]
        tbuf = [cAa[:, 20480:24576], cAa[:, 24576:28672]]
        qloads = []
        for i in range(16):
            for half in range(2):
                qloads.append((w1t[i * 128:(i + 1) * 128, half * 4096:(half + 1) * 4096],
                               w1q8[i * 128:(i + 1) * 128, half * 4096:(half + 1) * 4096],
                               4096, t_th1, t_nth1))
        for j in range(64):
            qloads.append((w2t[j * 128:(j + 1) * 128, :], w2q8[j * 128:(j + 1) * 128, :],
                           2048, t_th2, t_nth2))
        store_done = []
        tern_done = []
        for idx, (src_, dst, width, th, nth) in enumerate(qloads):
            bu = idx % 2
            if idx >= 2:
                P.wait("sp", "dve", tern_done[idx - 2])
            P.emit("sp", lambda e, s=src_, b=bu, w=width: e.dma_start(out=wbuf[b][:, 0:w], in_=s),
                   inc="spq", by=16)
            P.wait("dve", "spq")
            if idx >= 2:
                P.wait("dve", "poq", store_done[idx - 2])
            P.emit("dve", lambda e, b=bu, w=width, n=nth: e.tensor_scalar(
                ubuf[:, 0:w], wbuf[b][:, 0:w], n[:], None, ALU.is_lt))
            v = P.emit("dve", lambda e, b=bu, w=width, t=th: e.scalar_tensor_tensor(
                tbuf[b][:, 0:w], wbuf[b][:, 0:w], t[:], ubuf[:, 0:w], ALU.is_gt, ALU.subtract), inc="dve")
            tern_done.append(v)
            P.wait("pool", "dve")
            v = P.emit("pool", lambda e, d=dst, b=bu, w=width: e.dma_start(out=d, in_=tbuf[b][:, 0:w]),
                       inc="poq", by=16)
            store_done.append(v)
            yield

    def pump(gen, n):
        if gen is None:
            return
        for _ in range(n):
            try:
                next(gen)
            except StopIteration:
                break

    def drain(gen):
        if gen is None:
            return
        for _ in gen:
            pass

    def ph1_ln(b, pump_gen=None):
        P.barrier()
        xb = [cDa[:, 0:512], cDa[:, 512:1024]]
        xsq = cDa[:, 1024:1536]
        rstd_b = cDa[:, 1536:2048]
        mrs_b = cDa[:, 2048:2560]
        tmpf = cDa[:, 2560:3072]
        for tb in range(NTB):
            ts = slice(tb * 512, (tb + 1) * 512)
            for dc in range(NDC):
                pump(pump_gen, 1)
                bu = dc % 2
                P.wait("sp", "act")
                P.wait("sp", "pe")
                P.emit("sp", lambda e, b_=b, d=dc, t=ts, u=bu: e.dma_start(
                    out=xb[u], in_=xt[b_, d * 128:(d + 1) * 128, t]), inc="spq", by=16)
                P.wait("act", "spq")
                P.wait("act", "pe")
                va = P.emit("act", lambda e, u=bu: e.activation(xsq, xb[u], AF.Square), inc="act")
                P.wait("pe", "spq")
                if dc == 0:
                    P.wait("pe", "dve")
                P.emit("pe", lambda e, u=bu, d=dc: e.matmul(
                    PB[0:1, 0:512], c_onesf[:], xb[u], start=(d == 0), stop=(d == NDC - 1),
                    skip_group_check=True))
                P.wait("pe", "act", va)
                P.emit("pe", lambda e, d=dc: e.matmul(
                    PB[0:1, 512:1024], c_onesf[:], xsq, start=(d == 0), stop=(d == NDC - 1),
                    skip_group_check=True), inc="pe")
            P.wait("dve", "pe")
            P.emit("dve", lambda e: e.tensor_scalar(t_rows[0:1, 0:512], PB[0:1, 0:512], 1.0 / L, None, ALU.mult), inc="dve")
            P.wait("dve", "dve")
            P.emit("dve", lambda e: e.tensor_tensor(t_rows[0:1, 512:1024], t_rows[0:1, 0:512], t_rows[0:1, 0:512], ALU.mult))
            P.emit("dve", lambda e: e.tensor_scalar(t_rows[0:1, 1024:1536], PB[0:1, 512:1024], 1.0 / L, None, ALU.mult))
            P.emit("dve", lambda e: e.tensor_tensor(t_rows[0:1, 1024:1536], t_rows[0:1, 1024:1536], t_rows[0:1, 512:1024], ALU.subtract), inc="dve")
            P.wait("act", "dve")
            P.emit("act", lambda e: e.activation(t_rows[0:1, 1536:2048], t_rows[0:1, 1024:1536], AF.Ln, bias=t_eps1[:]))
            P.emit("act", lambda e: e.activation(t_rows[0:1, 1536:2048], t_rows[0:1, 1536:2048], AF.Exp, scale=-0.5), inc="act")
            P.wait("dve", "act")
            P.emit("dve", lambda e: e.tensor_tensor(t_rows[0:1, 512:1024], t_rows[0:1, 0:512], t_rows[0:1, 1536:2048], ALU.mult), inc="dve")
            P.wait("pe", "dve")
            P.emit("pe", lambda e: e.matmul(PB[:, 1024:1536], c_ones128[:], t_rows[0:1, 1536:2048], start=True, stop=True))
            P.emit("pe", lambda e: e.matmul(PB[:, 1536:2048], c_ones128[:], t_rows[0:1, 512:1024], start=True, stop=True), inc="pe")
            P.wait("dve", "pe")
            P.emit("dve", lambda e: e.tensor_copy(rstd_b, PB[:, 1024:1536]))
            P.emit("dve", lambda e: e.tensor_copy(mrs_b, PB[:, 1536:2048]), inc="dve")
            for dc in range(NDC):
                pump(pump_gen, 1)
                bu = dc % 2
                P.wait("sp", "dve")
                P.emit("sp", lambda e, b_=b, d=dc, t=ts, u=bu: e.dma_start(
                    out=xb[u], in_=xt[b_, d * 128:(d + 1) * 128, t]), inc="spq", by=16)
                P.wait("dve", "spq")
                P.emit("dve", lambda e, u=bu: e.tensor_tensor(tmpf, xb[u], rstd_b, ALU.mult))
                P.emit("dve", lambda e, d=dc, t=ts: e.tensor_tensor(
                    cAa[:, d * 2048 + t.start:d * 2048 + t.stop], tmpf, mrs_b, ALU.subtract), inc="dve")
            P.wait("sp", "dve")
            xn_view = cAa.rearrange("p (dc t) -> p dc t", dc=NDC)[:, :, ts]
            sp_view = xnsp[b].rearrange("(dc p) t -> p dc t", p=128)[:, :, ts]
            for q in range(4):
                P.emit("sp", lambda e, q=q, xv=xn_view, sv=sp_view: e.dma_start(
                    out=sv[:, q * 4:(q + 1) * 4, :], in_=xv[:, q * 4:(q + 1) * 4, :]), inc="spq", by=16)

    def ph1_v(b, pump_gen=None, fill_xn=False):
        P.barrier()
        if fill_xn:
            for dc in range(NDC):
                P.emit("sp", lambda e, b_=b, d=dc: e.dma_start(
                    out=cAa[:, d * 2048:(d + 1) * 2048], in_=xnsp[b_, d * 128:(d + 1) * 128, :]),
                    inc="spq", by=16)
            P.wait("pe", "spq")
            P.wait("dve", "spq")
        for tc in range(16):
            for kv in range(4):
                col = VA0 + tc * 520 + kv * 130 + 128
                P.emit("dve", lambda e, c=col: e.memset(cCa[:, c:c + 1], 1.0))
        P.emit("dve", lambda e: e.engine_nop(), inc="dve")
        vw = [cDb[:, 6144:6656], cDb[:, 6656:7168]]
        mm_done = []
        for half in range(2):
            for dc in range(NDC):
                pump(pump_gen, 1)
                bu = dc % 2
                gi = half * NDC + dc
                if gi >= 2:
                    P.wait("pool", "pe", mm_done[gi - 2])
                P.emit("pool", lambda e, d=dc, u=bu: e.dma_start(
                    out=vw[u], in_=wv[d * 128:(d + 1) * 128, :]), inc="poq", by=16)
                P.wait("pe", "poq")
                if dc == 0:
                    P.wait("pe", "dve")
                for tcl in range(8):
                    tc = half * 8 + tcl
                    v = P.emit("pe", lambda e, d=dc, t=tc, tl=tcl, u=bu: e.matmul(
                        bank(tl), cAa[:, d * 2048 + t * 128:d * 2048 + (t + 1) * 128],
                        vw[u], start=(d == 0), stop=(d == NDC - 1),
                        skip_group_check=True),
                        inc="pe" if tcl == 7 else None)
                    if tcl == 7:
                        mm_done.append(v)
            P.wait("dve", "pe")
            for tcl in range(8):
                tc = half * 8 + tcl
                dst = cCa[:, VA0 + tc * 520:VA0 + (tc + 1) * 520].rearrange(
                    "p (kv c) -> p kv c", kv=4)[:, :, 0:128]
                src = bank(tcl).rearrange("p (kv c) -> p kv c", kv=4)
                P.emit("dve", lambda e, d=dst, s=src: e.tensor_copy(d, s), inc="dve")

    def ph1_qk(b):
        P.barrier()
        wqb = [cDb[:, 0:2048], cDb[:, 2048:4096]]
        qcb = [[cDb[:, 4096 + p * 2048 + t * 512:4096 + p * 2048 + (t + 1) * 512] for t in range(4)] for p in range(2)]
        qrb = [[cDb[:, 8192 + p * 2048 + t * 512:8192 + p * 2048 + (t + 1) * 512] for t in range(4)] for p in range(2)]
        tmp2 = cDb[:, 12288:12800]
        sqt = cDb[:, 12800:13312]
        NE = H + HKV
        mm_end = {}
        paA_done = {}
        sin_done = {0: 0, 1: 0}
        red_done = 0
        for eci in range(NE + 1):
            if eci >= 1:
                ec = eci - 1
                par = ec % 2
                P.wait("dve", "pe")
                for tb in range(NTB):
                    ts0 = tb * 512
                    P.emit("dve", lambda e, t=tb, p=par, t0=ts0: e.tensor_tensor(
                        qcb[p][t], bank(t), c_cos[:, t0:t0 + 512], ALU.mult))
                    v = P.emit("dve", lambda e, t=tb, p=par: e.tensor_copy(qrb[p][t], bank(t)), inc="dve")
                paA_done[ec] = v
            if eci < NE:
                isq = eci < H
                w_src = wq if isq else wk
                ecs = eci if isq else eci - H
                bu = eci % 2
                wr = w_src.rearrange("(dc p) e -> p dc e", p=128)[:, :, ecs * 128:(ecs + 1) * 128]
                wv_ = wqb[bu].rearrange("p (dc e) -> p dc e", dc=NDC)
                P.wait("pool", "pe")
                ld = None
                for q in range(4):
                    ld = P.emit("pool", lambda e, q=q, w=wr, o=wv_: e.dma_start(
                        out=o[:, q * 4:(q + 1) * 4, :], in_=w[:, q * 4:(q + 1) * 4, :]), inc="poq", by=16)
                P.wait("pe", "poq", ld)
                if eci >= 1:
                    P.wait("pe", "dve", paA_done[eci - 1])
                for dc in range(NDC):
                    for tb in range(NTB):
                        v = P.emit("pe", lambda e, d=dc, t=tb, u=bu: e.matmul(
                            bank(t), wqb[u][:, d * 128:(d + 1) * 128],
                            cAa[:, d * 2048 + t * 512:d * 2048 + (t + 1) * 512],
                            start=(d == 0), stop=(d == NDC - 1), skip_group_check=True),
                            inc="pe" if (dc == NDC - 1 and tb == NTB - 1) else None)
                mm_end[eci] = v
            if eci >= 1:
                ec = eci - 1
                par = ec % 2
                isq_b = ec < H
                ecs_b = ec if isq_b else ec - H
                arr, base = (cBa, ec * 2048) if isq_b else (cCa, ecs_b * 2048)
                for tb in range(NTB):
                    ts0 = tb * 512
                    rb = 4 + tb % 2
                    P.wait("pe", "dve", max(paA_done[ec], sin_done[tb % 2]))
                    rv = P.emit("pe", lambda e, r=rb, p=par, t=tb: e.matmul(
                        bank(r), c_roll[:], qrb[p][t], start=True, stop=True,
                        skip_group_check=True), inc="pe")
                    P.wait("dve", "pe", rv)
                    P.emit("dve", lambda e, r=rb, t0=ts0: e.tensor_tensor(
                        tmp2, bank(r), c_sin[:, t0:t0 + 512], ALU.mult))
                    sin_done[tb % 2] = P.cnt["dve"] + 1
                    av_ = P.emit("dve", lambda e, a=arr, o=base + ts0, p=par, t=tb: e.tensor_tensor(
                        a[:, o:o + 512], tmp2, qcb[p][t], ALU.add), inc="dve")
        sq2 = [sqt, cDb[:, 13312:13824]]
        sq_val = {}
        mm_val = {}
        rd_val = {}
        strips = [(cBa, h * 2048, h * 4) for h in range(H)] + \
                 [(cCa, kv * 2048, 64 + kv * 4) for kv in range(HKV)]
        P.wait("act", "dve")
        items = [(arr, base + t * 512, n0 + t) for (arr, base, n0) in strips for t in range(NTB)]
        for i, (arr, off, nidx) in enumerate(items):
            su = i % 2
            if i >= 2:
                P.wait("act", "pe", mm_val[i - 2])
            sq_val[i] = P.emit("act", lambda e, a=arr, o=off, u=su: e.activation(
                sq2[u], a[:, o:o + 512], AF.Square), inc="act")
            P.wait("pe", "act", sq_val[i])
            if i >= 2:
                P.wait("pe", "dve", rd_val[i - 2])
            nb = 3072 if i % 2 == 0 else 3584
            mm_val[i] = P.emit("pe", lambda e, u=su, nb=nb: e.matmul(
                PB[0:1, nb:nb + 512], c_onescb[:], sq2[u], start=True, stop=True,
                skip_group_check=True), inc="pe")
            P.wait("dve", "pe", mm_val[i])
            rd_val[i] = P.emit("dve", lambda e, n=nidx, nb=nb: e.tensor_reduce(
                t_nacc[0:1, n:n + 1], PB[0:1, nb:nb + 512], AX.X, ALU.max), inc="dve")
        P.emit("dve", lambda e: e.tensor_reduce(t_nacc[0:1, 96:97], t_nacc[0:1, 0:64], AX.X, ALU.max))
        P.emit("dve", lambda e: e.tensor_reduce(t_nacc[0:1, 97:98], t_nacc[0:1, 64:80], AX.X, ALU.max), inc="dve")
        P.wait("dve", "dve")
        P.emit("dve", lambda e: e.tensor_tensor(t_nacc[0:1, 98:99], t_nacc[0:1, 96:97], t_nacc[0:1, 97:98], ALU.mult), inc="dve")
        P.wait("act", "dve")
        P.emit("act", lambda e: e.activation(t_nacc[0:1, 99:100], t_nacc[0:1, 98:99], AF.Sqrt, scale=1.0 / HD), inc="act")
        P.wait("pe", "act")
        P.emit("pe", lambda e: e.matmul(PB[:, 3072:3073], c_mones128[:], t_nacc[0:1, 99:100], start=True, stop=True), inc="pe")
        P.wait("dve", "pe")
        P.emit("dve", lambda e: e.tensor_copy(t_cbias[:], PB[:, 3072:3073]), inc="dve")

    def ph1_ln_bg(b):
        xb = [cBf[:, 0:512], cBf[:, 512:1024]]
        xsq = cBf[:, 1024:1536]
        rstd_b = cBf[:, 1536:2048]
        mrs_b = cBf[:, 2048:2560]
        tmpf = cBf[:, 2560:3072]
        xnt = [cBa[:, 8192:8704], cBa[:, 8704:9216]]
        rows = cBf[:, 3072:3584]
        mrow = cBf[0:1, 3072:3584]
        m2row = cBf[0:1, 3584:4096]
        varrow = cBf[0:1, 4096:4608]
        rstdrow = cBf[0:1, 4608:5120]
        sq_done = {}
        mm_done = {}
        xn_store = {}
        for tb in range(NTB):
            ts = slice(tb * 512, (tb + 1) * 512)
            for dc in range(NDC):
                bu = dc % 2
                P.wait("sp", "pe")
                P.emit("sp", lambda e, b_=b, d=dc, t=ts, u=bu: e.dma_start(
                    out=xb[u], in_=xt[b_, d * 128:(d + 1) * 128, t]), inc="spq", by=16)
                P.wait("dve", "spq")
                P.wait("dve", "pe")
                va = P.emit("dve", lambda e, u=bu: e.scalar_tensor_tensor(
                    xsq, xb[u], 0.0, xb[u], ALU.add, ALU.mult), inc="dve")
                P.wait("pe", "spq")
                if dc == 0:
                    P.wait("pe", "dve")
                P.emit("pe", lambda e, u=bu, d=dc: e.matmul(
                    PB[0:1, 1024:1536], c_onesf[:], xb[u], start=(d == 0), stop=(d == NDC - 1),
                    skip_group_check=True))
                P.wait("pe", "dve", va)
                P.emit("pe", lambda e, d=dc: e.matmul(
                    PB[0:1, 1536:2048], c_onesf[:], xsq, start=(d == 0), stop=(d == NDC - 1),
                    skip_group_check=True), inc="pe")
                yield
            P.wait("dve", "pe")
            P.emit("dve", lambda e: e.tensor_scalar(mrow, PB[0:1, 1024:1536], 1.0 / L, None, ALU.mult), inc="dve")
            P.wait("dve", "dve")
            P.emit("dve", lambda e: e.tensor_tensor(m2row, mrow, mrow, ALU.mult))
            P.emit("dve", lambda e: e.tensor_scalar(varrow, PB[0:1, 1536:2048], 1.0 / L, None, ALU.mult))
            P.emit("dve", lambda e: e.tensor_tensor(varrow, varrow, m2row, ALU.subtract), inc="dve")
            P.wait("act", "dve")
            P.emit("act", lambda e: e.activation(rstdrow, varrow, AF.Ln, bias=t_eps1[:]))
            P.emit("act", lambda e: e.activation(rstdrow, rstdrow, AF.Exp, scale=-0.5), inc="act")
            P.wait("dve", "act")
            P.emit("dve", lambda e: e.tensor_tensor(m2row, mrow, rstdrow, ALU.mult), inc="dve")
            P.wait("pe", "dve")
            P.emit("pe", lambda e: e.matmul(PB[:, 1024:1536], c_ones128[:], rstdrow, start=True, stop=True))
            P.emit("pe", lambda e: e.matmul(PB[:, 1536:2048], c_ones128[:], m2row, start=True, stop=True), inc="pe")
            P.wait("dve", "pe")
            P.emit("dve", lambda e: e.tensor_copy(rstd_b, PB[:, 1024:1536]))
            P.emit("dve", lambda e: e.tensor_copy(mrs_b, PB[:, 1536:2048]), inc="dve")
            yield
            for dc in range(NDC):
                bu = dc % 2
                xu = dc % 2
                P.wait("sp", "dve")
                P.emit("sp", lambda e, b_=b, d=dc, t=ts, u=bu: e.dma_start(
                    out=xb[u], in_=xt[b_, d * 128:(d + 1) * 128, t]), inc="spq", by=16)
                P.wait("dve", "spq")
                if dc >= 2:
                    P.wait("dve", "spq", xn_store[dc - 2])
                P.emit("dve", lambda e, u=bu: e.tensor_tensor(tmpf, xb[u], rstd_b, ALU.mult))
                v = P.emit("dve", lambda e, u=xu: e.tensor_tensor(xnt[u], tmpf, mrs_b, ALU.subtract), inc="dve")
                P.wait("sp", "dve", v)
                xn_store[dc] = P.emit("sp", lambda e, b_=b, d=dc, t=ts, u=xu: e.dma_start(
                    out=xnsp[b_, d * 128:(d + 1) * 128, t], in_=xnt[u]), inc="spq", by=16)
                yield

    def ph2(b, pump_gen=None):
        P.barrier()
        PT = cDb
        rb_sbuf = cDa[:, 4096:4608]
        t_prow = t_rows[0:1, 0:512]
        exp_after = {}
        av_evac = {0: 0, 1: 0}
        denom_free = 0
        bcast_free = 0
        for h in range(H):
            kv = h // 4
            for qb in range(NTB):
                q0 = qb * 512
                ab = 4 + (qb % 2)
                av_done = den_done = None
                exp_val = {}
                for g in range(9):
                    if g < 8:
                        bs = g % 2
                        b0 = bs * 2
                        P.wait("pe", "act", exp_after.get(bs, 0))
                        for ki in range(2):
                            kc = g * 2 + ki
                            P.emit("pe", lambda e, k=kc, v=kv, bk=b0 + ki, q=q0, h=h: e.matmul(
                                bank(bk), cCa[:, v * 2048 + k * 128:v * 2048 + (k + 1) * 128],
                                cBa[:, h * 2048 + q:h * 2048 + q + 512],
                                start=True, stop=True, skip_group_check=True), inc="pe")
                        P.wait("act", "pe")
                        v = P.emit("act", lambda e, g=g, b0=b0: e.activation(
                            PT[:, g * 1024:(g + 1) * 1024], PB[:, b0 * 512:(b0 + 2) * 512],
                            AF.Exp, scale=ISQ, bias=t_cbias[:]), inc="act")
                        exp_after[bs] = v
                        exp_val[g] = v
                    if g >= 1:
                        ga = g - 1
                        P.wait("pe", "act", exp_val[ga])
                        if ga == 0:
                            P.wait("pe", "dve", max(av_evac[qb % 2], denom_free))
                        for ki in range(2):
                            kc = ga * 2 + ki
                            av_done = P.emit("pe", lambda e, k=kc, v=kv, a=ab: e.matmul(
                                PB[:, a * 512:(a + 1) * 512],
                                cCa[:, VA0 + k * 520 + v * 130:VA0 + k * 520 + v * 130 + 128],
                                PT[:, k * 512:(k + 1) * 512],
                                start=(k == 0), stop=(k == 15), skip_group_check=True),
                                inc="pe" if kc == 15 else None)
                            den_done = P.emit("pe", lambda e, k=kc, v=kv: e.matmul(
                                PB[0:1, 3072:3584],
                                cCa[:, VA0 + k * 520 + v * 130 + 128:VA0 + k * 520 + v * 130 + 129],
                                PT[:, k * 512:(k + 1) * 512],
                                start=(k == 0), stop=(k == 15), skip_group_check=True),
                                inc="pe" if kc == 15 else None)
                pump(pump_gen, 2)
                P.wait("dve", "pe", den_done)
                denom_free = P.emit("dve", lambda e: e.reciprocal(t_prow, PB[0:1, 3072:3584]), inc="dve")
                P.wait("pe", "dve", max(denom_free, bcast_free))
                bc_done = P.emit("pe", lambda e: e.matmul(
                    PB[:, 3584:4096], c_ones128[:], t_prow, start=True, stop=True), inc="pe")
                P.wait("dve", "pe", bc_done)
                P.emit("dve", lambda e: e.tensor_copy(rb_sbuf, PB[:, 3584:4096]), inc="dve")
                bcast_free = P.cnt["dve"]
                P.wait("dve", "pe", av_done)
                P.wait("dve", "dve")
                av_evac[qb % 2] = P.emit("dve", lambda e, a=ab, h=h, q=q0: e.tensor_tensor(
                    cBa[:, h * 2048 + q:h * 2048 + q + 512], PB[:, a * 512:(a + 1) * 512],
                    rb_sbuf, ALU.mult), inc="dve")
        if debug and b == dbg_b:
            P.barrier()
            P.emit("sp", lambda e: e.dma_start(out=d_qrot, in_=cBa[:, :]), inc="spq", by=16)
            P.emit("sp", lambda e: e.dma_start(out=d_kv, in_=cCa[:, :]), inc="spq", by=16)
            P.emit("sp", lambda e: e.dma_start(out=d_misc, in_=t_cbias[:]), inc="spq", by=16)

    def ph25(b, pump_gen=None):
        P.barrier()
        wob = [cDb[:, 0:2048], cDb[:, 2048:4096]]
        stile = [cDb[:, 4096:4608], cDb[:, 4608:5120]]
        mm_last = {}
        ld_last = {}
        evac_last = {}
        store_last = {}
        for dc in range(NDC):
            bu = dc % 2
            wr = wo.rearrange("(ec p) d -> p ec d", p=128)[:, :, dc * 128:(dc + 1) * 128]
            wv_ = wob[bu].rearrange("p (ec d) -> p ec d", ec=16)
            if dc >= 2:
                P.wait("pool", "pe", mm_last[dc - 2])
            for q in range(4):
                ld_last[dc] = P.emit("pool", lambda e, q=q, w=wr, o=wv_: e.dma_start(
                    out=o[:, q * 4:(q + 1) * 4, :], in_=w[:, q * 4:(q + 1) * 4, :]), inc="poq", by=16)
            P.wait("pe", "poq", ld_last[dc])
            if dc >= 1:
                for t in range(NTB):
                    P.wait("pe", "dve", evac_last[(dc - 1, t)])
            mm_vals = {}
            for tb in range(NTB):
                for ec in range(16):
                    v = P.emit("pe", lambda e, t=tb, c=ec, u=bu: e.matmul(
                        bank(t), wob[u][:, c * 128:(c + 1) * 128],
                        cBa[:, c * 2048 + t * 512:(c) * 2048 + (t + 1) * 512],
                        start=(c == 0), stop=(c == 15), skip_group_check=True),
                        inc="pe" if ec == 15 else None)
                    if v is not None:
                        mm_vals[tb] = v
            mm_last[dc] = mm_vals[NTB - 1]
            pump(pump_gen, 6)
            for tb in range(NTB):
                su = tb % 2
                P.wait("dve", "pe", mm_vals[tb])
                if tb >= 2:
                    P.wait("dve", "spq", store_last[(dc, tb - 2)])
                evac_last[(dc, tb)] = P.emit("dve", lambda e, t=tb, u=su: e.tensor_copy(stile[u], bank(t)), inc="dve")
                P.wait("sp", "dve", evac_last[(dc, tb)])
                store_last[(dc, tb)] = P.emit("sp", lambda e, b_=b, d=dc, t=tb, u=su: e.dma_start(
                    out=ssp[b_, d * 128:(d + 1) * 128, t * 512:(t + 1) * 512], in_=stile[u]),
                    inc="spq", by=16)

    def ph3(b, pump_gen=None):
        P.barrier()
        xn_tb = cDb[:, 0:8192]
        gtile = [cDb[:, 8192:8704], cDb[:, 8704:9216]]
        sti = [cDb[:, 9216:9728], cDb[:, 9728:10240]]
        xg = [cDa[:, 5120:5632], cDa[:, 5632:6144]]
        sgt = cDa[:, 6144:6656]
        otile = cDa[:, 6656:7168]
        chunk = [cCa[:, 0:8192], cCa[:, 8192:16384]]
        for tb in range(NTB):
            ts0 = tb * 512
            P.wait("sp", "pe")
            xv = xnsp[b].rearrange("(dc p) t -> p dc t", p=128)[:, :, ts0:ts0 + 512]
            ov = xn_tb.rearrange("p (dc t) -> p dc t", dc=NDC)
            for q in range(4):
                P.emit("sp", lambda e, q=q, x=xv, o=ov: e.dma_start(
                    out=o[:, q * 4:(q + 1) * 4, :], in_=x[:, q * 4:(q + 1) * 4, :]), inc="spq", by=16)
            w1r = w1q8.rearrange("(dc p) f -> p dc f", p=128)
            gelu_done = {}
            ld_done = {}
            for fch in range(16):
                pump(pump_gen, 1)
                bu = fch % 2
                if fch >= 2:
                    P.wait("pool", "pe", ld_done.get(fch - 2, 0))
                else:
                    P.wait("pool", "pe")
                cv = chunk[bu].rearrange("p (dc f) -> p dc f", dc=NDC)
                for q in range(4):
                    P.emit("pool", lambda e, q=q, f=fch, c=cv: e.dma_start(
                        out=c[:, q * 4:(q + 1) * 4, :],
                        in_=w1r[:, q * 4:(q + 1) * 4, f * 512:(f + 1) * 512]), inc="poq", by=16)
                P.wait("pe", "poq")
                P.wait("pe", "spq")
                for fcs in range(4):
                    fc = fch * 4 + fcs
                    bk = fc % 2
                    P.wait("pe", "act", gelu_done.get(bk, 0))
                    for dc in range(NDC):
                        v = P.emit("pe", lambda e, d=dc, s=fcs, k=bk, u=bu: e.matmul(
                            bank(k), chunk[u].rearrange("p (dc f) -> p dc f", dc=NDC)[:, d, s * 128:(s + 1) * 128],
                            xn_tb[:, d * 512:(d + 1) * 512],
                            start=(d == 0), stop=(d == NDC - 1), skip_group_check=True),
                            inc="pe" if dc == NDC - 1 else None)
                        if v is not None:
                            ld_done[fch] = v
                    P.wait("act", "pe")
                    gelu_done[bk] = P.emit("act", lambda e, f=fc, k=bk: e.activation(
                        cAa[:, f * 512:(f + 1) * 512], bank(k), AF.Gelu,
                        bias=c_b1p[:, f:f + 1]), inc="act")
            if debug and b == dbg_b and tb == 0:
                P.wait("sp", "act")
                P.emit("sp", lambda e: e.dma_start(out=d_h, in_=cAa[:, :]), inc="spq", by=16)
            w2r = w2q8.rearrange("(fc p) d -> p fc d", p=128)
            ld2_done = {}
            for dg in range(4):
                for f2c in range(4):
                    pump(pump_gen, 2)
                    bu = f2c % 2
                    gi = dg * 4 + f2c
                    if gi >= 2:
                        P.wait("pool", "pe", ld2_done[gi - 2])
                    else:
                        P.wait("pool", "pe")
                    cv = chunk[bu].rearrange("p (fcs d) -> p fcs d", fcs=16)
                    for q in range(4):
                        P.emit("pool", lambda e, q=q, f=f2c, g=dg, c=cv: e.dma_start(
                            out=c[:, q * 4:(q + 1) * 4, :],
                            in_=w2r[:, f * 16 + q * 4:f * 16 + (q + 1) * 4, g * 512:(g + 1) * 512]),
                            inc="poq", by=16)
                    P.wait("pe", "poq")
                    if f2c == 0:
                        P.wait("pe", "act")
                    for fcs in range(16):
                        fcg = f2c * 16 + fcs
                        for di in range(4):
                            v = P.emit("pe", lambda e, s=fcs, i=di, g=fcg, u=bu: e.matmul(
                                bank(4 + i), chunk[u].rearrange("p (fcs d) -> p fcs d", fcs=16)[:, s, i * 128:(i + 1) * 128],
                                cAa[:, g * 512:(g + 1) * 512],
                                start=(g == 0), stop=(g == 63), skip_group_check=True),
                                inc="pe" if (fcs == 15 and di == 3) else None)
                            if v is not None:
                                ld2_done[gi] = v
                for di in range(4):
                    dc = dg * 4 + di
                    gu = di % 2
                    P.wait("sp", "dve")
                    P.emit("sp", lambda e, b_=b, d=dc, t=ts0, u=gu: e.dma_start(
                        out=xg[u], in_=xt[b_, d * 128:(d + 1) * 128, t:t + 512]), inc="spq", by=16)
                    P.emit("sp", lambda e, b_=b, d=dc, t=ts0, u=gu: e.dma_start(
                        out=sti[u], in_=ssp[b_, d * 128:(d + 1) * 128, t:t + 512]), inc="spq", by=16)
                    P.wait("act", "pe")
                    if di >= 2:
                        P.wait("act", "dve")
                    P.emit("act", lambda e, d=dc, i=di, u=gu: e.activation(
                        gtile[u], bank(4 + i), AF.Silu, bias=c_b2p[:, d:d + 1]), inc="act")
                    if debug and b == dbg_b and tb == 0:
                        P.wait("sp", "act")
                        P.emit("sp", lambda e, d=dc, u=gu: e.dma_start(
                            out=d_g[:, d * 512:(d + 1) * 512], in_=gtile[u]), inc="spq", by=16)
                        P.wait("dve", "spq")
                    P.wait("dve", "act")
                    P.wait("dve", "spq")
                    P.emit("dve", lambda e, u=gu: e.tensor_tensor(sgt, gtile[u], sti[u], ALU.mult))
                    P.emit("dve", lambda e, u=gu: e.tensor_tensor(otile, sgt, xg[u], ALU.add), inc="dve")
                    P.wait("sp", "dve")
                    P.emit("sp", lambda e, b_=b, d=dc, t=ts0: e.dma_start(
                        out=outt[b_, d * 128:(d + 1) * 128, t:t + 512], in_=otile), inc="spq", by=16)

    ph0()
    mg = phA_mean_gen()
    ph1_ln(0, pump_gen=mg)
    ph1_v(0, pump_gen=mg)
    drain(mg)
    ph1_qk(0)
    phA_th()
    qg = phA_quant_gen()
    ph2(0, pump_gen=qg)
    ph25(0, pump_gen=qg)
    drain(qg)
    lg = ph1_ln_bg(1)
    ph3(0, pump_gen=lg)
    drain(lg)
    ph1_v(1, fill_xn=True)
    ph1_qk(1)
    ph2(1)
    ph25(1)
    ph3(1)
    P.barrier()

    with nc.Block() as block:
        @block.sync
        def _(e):
            for f in P.q["sp"]:
                f(e)

        @block.tensor
        def _(e):
            for f in P.q["pe"]:
                f(e)

        @block.scalar
        def _(e):
            for f in P.q["act"]:
                f(e)

        @block.vector
        def _(e):
            for f in P.q["dve"]:
                f(e)

        @block.gpsimd
        def _(e):
            for f in P.q["pool"]:
                f(e)

    ctx.close()
    return nc


def _host_constants():
    pos = np.arange(L, dtype=np.float64)
    inv_freq = 1.0 / (10000.0 ** (np.arange(0, HD, 2, dtype=np.float64) / HD))
    freqs = pos[:, None] * inv_freq[None, :]
    emb = np.concatenate([freqs, freqs], axis=-1)
    cos_fm = np.cos(emb).T.astype(ml_dtypes.bfloat16)
    sin_fm = np.sin(emb).T.astype(ml_dtypes.bfloat16)
    roll = np.zeros((128, 128), dtype=ml_dtypes.bfloat16)
    for j in range(128):
        roll[j, (j + 1) % 128] = 1.0
    return {
        "cosr": np.ascontiguousarray(cos_fm),
        "sinr": np.ascontiguousarray(sin_fm),
        "rollm": roll,
        "eyef": np.eye(128, dtype=np.float32),
        "ones128f": np.ones((1, 128), dtype=np.float32),
        "mones128f": np.full((1, 128), -1.0, dtype=np.float32),
        "onescb": np.ones((128, 1), dtype=ml_dtypes.bfloat16),
        "onesf32": np.ones((128, 1), dtype=np.float32),
        "sc24": np.full((128, 1), 2.0 ** -24, dtype=np.float32),
    }


def kernel(L0_attn_x, L0_attn_Wq, L0_attn_Wk, L0_attn_Wv, L0_attn_Wo,
           L0_ffn_w1, L0_ffn_b1, L0_ffn_w2, L0_ffn_b2):
    n_cores = 8
    x = np.asarray(L0_attn_x)
    consts = _host_constants()
    common = {
        "wq": np.ascontiguousarray(L0_attn_Wq, dtype=np.float32),
        "wk": np.ascontiguousarray(L0_attn_Wk, dtype=np.float32),
        "wv": np.ascontiguousarray(L0_attn_Wv, dtype=np.float32),
        "wo": np.ascontiguousarray(L0_attn_Wo, dtype=np.float32),
        "w1t": np.ascontiguousarray(np.asarray(L0_ffn_w1).T),
        "w2t": np.ascontiguousarray(np.asarray(L0_ffn_w2).T),
        "b1p": np.ascontiguousarray(np.asarray(L0_ffn_b1).reshape(NFC, 128).T),
        "b2p": np.ascontiguousarray(np.asarray(L0_ffn_b2).reshape(NDC, 128).T),
        **consts,
    }
    in_maps = []
    for c in range(n_cores):
        xs = x[c * B_CORE:(c + 1) * B_CORE]
        xtc = np.ascontiguousarray(np.transpose(xs, (0, 2, 1)))
        in_maps.append({"xt": xtc, **common})

    nc = build_program(debug=DEBUG)
    res = run_bass_kernel_spmd(nc, in_maps, core_ids=list(range(n_cores)))

    out = np.empty((16, L, D), dtype=np.float32)
    for c in range(n_cores):
        ot = res.results[c]["outt"]
        out[c * B_CORE:(c + 1) * B_CORE] = np.transpose(ot, (0, 2, 1))
    return out
